# version 02 (final)
# baseline (speedup 1.0000x reference)
from contextlib import ExitStack
import numpy as np
import concourse.bass as bass
import concourse.mybir as mybir
from concourse.bass_utils import run_bass_kernel_spmd

F32 = mybir.dt.float32
BF16 = mybir.dt.bfloat16
AF = mybir.ActivationFunctionType
ALU = mybir.AluOpType
AX = mybir.AxisListType

D = 2048
INTER = 4096
NH, HD, NG, GS = 64, 64, 8, 8
CONV_DIM = 6144
HSF = 2816
HEF = 1408
NEX = 10
EPS_LN = 1e-5
EPS_RMS = 1e-6


def build_program(T=2048, taps=False, phases=99):
    NT = T // 128
    TH = T + 256
    HT = TH // 128
    NCH = NT
    KD = D // 128
    NHS = INTER // 128
    NOC = CONV_DIM // 128
    NJ = HEF // 128
    NW = T // 512
    nc = bass.Bass("TRN2", target_bir_lowering=False)

    xpad = nc.dram_tensor("xpad", [TH, D], F32, kind="ExternalInput").ap()
    w_gate = nc.dram_tensor("w_gate", [D, INTER], F32, kind="ExternalInput").ap()
    w_hbc = nc.dram_tensor("w_hbc", [D, CONV_DIM], F32, kind="ExternalInput").ap()
    w_dt = nc.dram_tensor("w_dt", [D, NH], F32, kind="ExternalInput").ap()
    w_out = nc.dram_tensor("w_out", [INTER, D], F32, kind="ExternalInput").ap()
    w_up10 = nc.dram_tensor("w_up10", [NEX, D, 2 * HEF], F32, kind="ExternalInput").ap()
    w_dn10 = nc.dram_tensor("w_dn10", [NEX, HEF, D], F32, kind="ExternalInput").ap()
    w_gw = nc.dram_tensor("w_gw", [D, 8], F32, kind="ExternalInput").ap()
    conv_wt = nc.dram_tensor("conv_wt", [CONV_DIM, 4], F32, kind="ExternalInput").ap()
    conv_bc = nc.dram_tensor("conv_bc", [CONV_DIM, 1], F32, kind="ExternalInput").ap()
    dt_bias = nc.dram_tensor("dt_bias", [NH, 1], F32, kind="ExternalInput").ap()
    a_log = nc.dram_tensor("a_log", [NH, 1], F32, kind="ExternalInput").ap()
    d_row = nc.dram_tensor("d_row", [1, INTER], F32, kind="ExternalInput").ap()
    nw_cols = nc.dram_tensor("nw_cols", [128, NHS], F32, kind="ExternalInput").ap()
    gb_row = nc.dram_tensor("gb_row", [1, 8], F32, kind="ExternalInput").ap()
    ones_row = nc.dram_tensor("ones_row", [1, 128], F32, kind="ExternalInput").ap()
    incl_col = nc.dram_tensor("incl_col", [8, 1], F32, kind="ExternalInput").ap()
    tm_T = nc.dram_tensor("tm_T", [8, 8], F32, kind="ExternalInput").ap()
    out = nc.dram_tensor("out", [T, D], F32, kind="ExternalOutput").ap()

    hsT_scr = nc.dram_tensor("hsT_scr", [NHS, 128, T], BF16).ap()
    hs_scr = nc.dram_tensor("hs_scr", [NT, NHS, 128, 128], BF16).ap()
    bc_scr = nc.dram_tensor("bc_scr", [16, 128, T], BF16).ap()
    sg_scr = nc.dram_tensor("sg_scr", [NT, 128, INTER], BF16).ap()
    ynt_scr = nc.dram_tensor("ynt_scr", [NCH, NHS, 128, 128], BF16).ap()
    xnt_scr = nc.dram_tensor("xnt_scr", [KD, 128, T], BF16).ap()
    acum_scr = nc.dram_tensor("acum_scr", [NH, T], F32).ap()
    tott_scr = nc.dram_tensor("tott_scr", [NCH, NH], F32).ap()
    wfac_scr = nc.dram_tensor("wfac_scr", [8, NH], F32).ap()
    acc_scr = nc.dram_tensor("acc_scr", [T, D], F32).ap()
    agin = nc.dram_tensor("agin", [129, NH * HD], F32)
    agout = nc.dram_tensor("agout", [8 * 129, NH * HD], F32)

    tp = {}
    if taps:
        def tapt(name, shape, dt=F32):
            tp[name] = nc.dram_tensor(name, shape, dt, kind="ExternalOutput").ap()
        tapt("t_xnt", [KD, 128, TH], BF16)
        tapt("t_sg", [NT, 128, INTER], BF16)
        tapt("t_hs", [NT, NHS, 128, 128], BF16)
        tapt("t_bc", [16, 128, T], BF16)
        tapt("t_acum", [NH, T])
        tapt("t_dec", [128, NCH * NH])
        tapt("t_esc", [128, NCH * NH])
        tapt("t_sloc", [128, NH * HD])
        tapt("t_sglob", [128, NH * HD])
        tapt("t_y", [NCH, 128, INTER])
        tapt("t_acc3", [T, D])
        tapt("t_ynt", [NCH, NHS, 128, 128], BF16)
        tapt("t_w2h0", [128, NHS * 1024], BF16)
        tapt("t_w2h1", [128, NHS * 1024], BF16)
        tapt("t_we", [128, NT * NEX])
        tapt("t_acc4", [T, D])

    N = {"qg": 0, "qs": 0, "te": 0, "dv": 0, "ac": 0, "cc": 0}
    max_waited = {k: 0 for k in N}

    _es1 = ExitStack()
    s_qg = _es1.enter_context(nc.semaphore("qg"))
    s_qs = _es1.enter_context(nc.semaphore("qs"))
    s_te = _es1.enter_context(nc.semaphore("te"))
    s_dv = _es1.enter_context(nc.semaphore("dv"))
    s_ac = _es1.enter_context(nc.semaphore("ac"))
    s_cc = _es1.enter_context(nc.semaphore("cc"))
    s_db = _es1.enter_context(nc.semaphore("db"))
    _pool = [_es1.enter_context(nc.semaphore("sp%d" % i)) for i in range(56)]
    id_bf = _es1.enter_context(nc.sbuf_tensor("id_bf", [128, 128], BF16))
    id_f32 = _es1.enter_context(nc.sbuf_tensor("id_f32", [128, 128], F32))
    mask_triu = _es1.enter_context(nc.sbuf_tensor("mask_triu", [128, 128], F32))
    ones_sb = _es1.enter_context(nc.sbuf_tensor("ones_sb", [1, 128], F32))
    nw_sb = _es1.enter_context(nc.sbuf_tensor("nw_sb", [128, NHS], F32))
    bias_bc = _es1.enter_context(nc.sbuf_tensor("bias_bc", [128, 8], F32))
    we_sb = _es1.enter_context(nc.sbuf_tensor("we_sb", [128, 16 * NEX], F32))
    epsln_c = _es1.enter_context(nc.sbuf_tensor("epsln_c", [128, 1], F32))
    epsrm_c = _es1.enter_context(nc.sbuf_tensor("epsrm_c", [128, 1], F32))
    block = _es1.enter_context(nc.Block())
    SEM = {"qg": s_qg, "qs": s_qs, "te": s_te, "dv": s_dv, "ac": s_ac, "cc": s_cc}
    _pool_free = list(_pool)

    def _sem(name):
        if name not in SEM:
            assert _pool_free, "stream semaphore pool exhausted"
            SEM[name] = _pool_free.pop(0)
            N[name] = 0
            max_waited[name] = 0
        return SEM[name]

    def W(eng, name, val):
        if val <= 0:
            return
        _sem(name)
        max_waited[name] = max(max_waited[name], val)
        eng.wait_ge(SEM[name], val)

    def I(instr, name, k=1):
        instr.then_inc(_sem(name), k)
        N[name] += k
        return N[name]

    def WALL(eng, *names):
        for nm in names:
            W(eng, nm, N.get(nm, 0))

    def phase_barrier():
        @block.vector
        def _(v):
            W(v, "dv", N["dv"])
            I(v.engine_nop(), "dv")
        te_b, dv_b, ac_b = N["te"], N["dv"], N["ac"]
        dma_streams = [nm for nm in N if nm not in ("te", "dv", "ac", "cc")]

        @block.gpsimd
        def _(g):
            W(g, "te", te_b); W(g, "dv", dv_b); W(g, "ac", ac_b)
            WALL(g, *dma_streams)

        @block.sync
        def _(sy):
            W(sy, "te", te_b); W(sy, "dv", dv_b); W(sy, "ac", ac_b)
            WALL(sy, *dma_streams)

        @block.scalar
        def _(s):
            W(s, "te", te_b); W(s, "dv", dv_b)
            WALL(s, *dma_streams)

        @block.vector
        def _(v):
            W(v, "te", te_b); W(v, "ac", ac_b)
            WALL(v, *dma_streams)

        @block.tensor
        def _(t):
            W(t, "dv", dv_b); W(t, "ac", ac_b)
            WALL(t, *dma_streams)

    with nc.psum_tensor("ps_c0", [128, 512], F32) as ps_c0:
        @block.gpsimd
        def _(g):
            g.memset(id_bf[:], 0.0)
            g.affine_select(out=id_bf[:], in_=id_bf[:], compare_op=ALU.not_equal,
                            fill=1.0, base=0, pattern=[[-1, 128]], channel_multiplier=1)
            g.memset(id_f32[:], 0.0)
            g.affine_select(out=id_f32[:], in_=id_f32[:], compare_op=ALU.not_equal,
                            fill=1.0, base=0, pattern=[[-1, 128]], channel_multiplier=1)
            g.memset(epsln_c[:], EPS_LN)
            g.memset(epsrm_c[:], EPS_RMS)
            g.memset(mask_triu[:], 1.0)
            I(g.affine_select(out=mask_triu[:], in_=mask_triu[:], compare_op=ALU.is_ge,
                              fill=0.0, base=0, pattern=[[1, 128]], channel_multiplier=-1),
              "cst")
            W(g, "cst", N["cst"])
            I(g.engine_nop(), "cst")
            I(g.dma_start(ones_sb[:], ones_row), "c0", 16)
            I(g.dma_start(nw_sb[:], nw_cols), "c0", 16)

        @block.gpsimd
        def _(g):
            I(g.dma_start(bias_bc[:], bass.AP(gb_row.tensor, 0, [[0, 128], [1, 8]])), "c0", 16)

    _es2 = ExitStack()
    esc_rep = _es2.enter_context(nc.sbuf_tensor("esc_rep", [128, NCH * NH], F32))
    decayT = _es2.enter_context(nc.sbuf_tensor("decayT", [128, NCH * NH], F32))
    dtT = _es2.enter_context(nc.sbuf_tensor("dtT", [128, NCH * NH], F32))
    e_chunk = _es2.enter_context(nc.sbuf_tensor("e_chunk", [128, NCH * NH], F32))
    acumT = _es2.enter_context(nc.sbuf_tensor("acumT", [128, NCH * NH], F32))
    s_state = _es2.enter_context(nc.sbuf_tensor("s_state", [128, NH * HD], F32))
    suma_row = _es2.enter_context(nc.sbuf_tensor("suma_row", [1, NH], F32))
    _esx = ExitStack()
    xnT = _esx.enter_context(nc.sbuf_tensor("xnT", [128, KD * TH], BF16))
    xv = xnT[:].rearrange("p (k t) -> p k t", k=KD)

    _es6 = ExitStack()
    xt0 = _es6.enter_context(nc.sbuf_tensor("xt0", [128, D], F32))
    xt1 = _es6.enter_context(nc.sbuf_tensor("xt1", [128, D], F32))
    xn0 = _es6.enter_context(nc.sbuf_tensor("xn0", [128, D], BF16))
    xn1 = _es6.enter_context(nc.sbuf_tensor("xn1", [128, D], BF16))
    sq_tr = _es6.enter_context(nc.sbuf_tensor("sq_tr", [128, D], BF16))
    ln_sc = _es6.enter_context(nc.sbuf_tensor("ln_sc", [128, 16], F32))
    ps_t0 = _es6.enter_context(nc.psum_tensor("ps_t0", [128, 128], BF16))
    ps_t1 = _es6.enter_context(nc.psum_tensor("ps_t1", [128, 128], BF16))
    ps_t2 = _es6.enter_context(nc.psum_tensor("ps_t2", [128, 128], BF16))
    ps_t3 = _es6.enter_context(nc.psum_tensor("ps_t3", [128, 128], BF16))
    xts, xns = [xt0, xt1], [xn0, xn1]
    pst = [ps_t0, ps_t1, ps_t2, ps_t3]
    qs0, dv0, ac0, te0 = N["qs"], N["dv"], N["ac"], N["te"]
    ACT_ = 10
    dv_stat = lambda tt: dv0 + 2 * (tt + 1) + KD * max(0, tt - 1)
    def dv_ev(gsl):
        tt, k = divmod(gsl, KD)
        return dv0 + 2 * min(tt + 2, HT) + KD * tt + k + 1

    @block.sync
    def _(sy):
        for tt in range(HT):
            if tt >= 2:
                W(sy, "ac", ac0 + ACT_ * (tt - 1))
            I(sy.dma_start(xts[tt % 2][:], xpad[tt * 128:(tt + 1) * 128, :]),
              "p0a" if tt % 2 == 0 else "p0b", 16)

    @block.vector
    def _(v):
        def stats(tt):
            W(v, "p0a" if tt % 2 == 0 else "p0b", 16 * (tt // 2 + 1))
            if tt >= 1:
                W(v, "ac", ac0 + ACT_ * tt)
            c8 = (tt % 2) * 8
            xt = xts[tt % 2]
            v.reduce_sum(ln_sc[:, c8 + 0:c8 + 1], xt[:], AX.X)
            I(v.scalar_tensor_tensor(sq_tr[:], xt[:], 1.0, xt[:], ALU.mult, ALU.mult,
                                     accum_out=ln_sc[:, c8 + 1:c8 + 2]), "dv")
            W(v, "dv", N["dv"])
            I(v.engine_nop(), "dv")
            assert N["dv"] == dv_stat(tt), (N["dv"], dv_stat(tt), tt)

        def evacs(tt):
            for k in range(KD):
                gsl = tt * KD + k
                W(v, "te", te0 + gsl + 1)
                I(v.tensor_copy(xv[:, k, tt * 128:(tt + 1) * 128], pst[gsl % 4][:]), "dv")
                assert N["dv"] == dv_ev(gsl), (N["dv"], dv_ev(gsl), gsl)

        for u in range(HT + 2):
            if u >= 2:
                evacs(u - 2)
            if u < HT:
                stats(u)

    @block.scalar
    def _(s):
        W(s, "cst", 2)
        for tt in range(HT):
            W(s, "dv", dv_stat(tt))
            if tt >= 2:
                W(s, "te", te0 + (tt - 1) * KD)
            c8 = (tt % 2) * 8
            ns = [ac0 + ACT_ * tt]
            def step(i):
                i.then_inc(SEM["ac"], 1)
                N["ac"] += 1
                ns[0] += 1
                s.wait_ge(SEM["ac"], ns[0])
            step(s.mul(ln_sc[:, c8 + 2:c8 + 3], ln_sc[:, c8 + 0:c8 + 1], 1.0 / D))
            step(s.activation(ln_sc[:, c8 + 3:c8 + 4], ln_sc[:, c8 + 1:c8 + 2], AF.Identity,
                              scale=1.0 / D, bias=epsln_c[:, 0:1]))
            step(s.mul(ln_sc[:, c8 + 4:c8 + 5], ln_sc[:, c8 + 2:c8 + 3], ln_sc[:, c8 + 2:c8 + 3]))
            step(s.activation(ln_sc[:, c8 + 5:c8 + 6], ln_sc[:, c8 + 4:c8 + 5], AF.Identity,
                              scale=-1.0, bias=ln_sc[:, c8 + 3:c8 + 4]))
            step(s.activation(ln_sc[:, c8 + 6:c8 + 7], ln_sc[:, c8 + 5:c8 + 6], AF.Ln))
            step(s.activation(ln_sc[:, c8 + 6:c8 + 7], ln_sc[:, c8 + 6:c8 + 7], AF.Exp, scale=-0.5))
            step(s.mul(ln_sc[:, c8 + 7:c8 + 8], ln_sc[:, c8 + 2:c8 + 3], ln_sc[:, c8 + 6:c8 + 7]))
            step(s.mul(ln_sc[:, c8 + 7:c8 + 8], ln_sc[:, c8 + 7:c8 + 8], -1.0))
            step(s.activation(xns[tt % 2][:], xts[tt % 2][:], AF.Identity,
                              bias=ln_sc[:, c8 + 7:c8 + 8], scale=ln_sc[:, c8 + 6:c8 + 7]))
            s.sem_inc(SEM["ac"], ac0 + ACT_ * (tt + 1) - ns[0])
            N["ac"] += ac0 + ACT_ * (tt + 1) - ns[0]

    @block.tensor
    def _(t):
        W(t, "cst", 2)
        for tt in range(HT):
            W(t, "ac", ac0 + ACT_ * (tt + 1))
            for k in range(KD):
                gsl = tt * KD + k
                if gsl >= 4:
                    W(t, "dv", dv_ev(gsl - 4))
                I(t.transpose(pst[gsl % 4][:], xns[tt % 2][:, k * 128:(k + 1) * 128],
                              id_bf[:]), "te")

    @block.vector
    def _(v):
        W(v, "dv", N["dv"])
        I(v.engine_nop(), "dv")

    @block.sync
    def _(sy):
        W(sy, "dv", N["dv"])
        for k in range(KD):
            I(sy.dma_start(xnt_scr[k], xv[:, k, 128:128 + T]), "xsp", 16)
        if taps:
            I(sy.dma_start(tp["t_xnt"].rearrange("k p t -> p k t"), xv[:]), "tap", 16)

    _es6.close()
    if phases >= 2:
        phase_barrier()
        _es7 = ExitStack()
        w1g = _es7.enter_context(nc.sbuf_tensor("w1g", [128, KD * 2048], BF16))
        sgst0 = _es7.enter_context(nc.sbuf_tensor("sgst0", [128, 2048], BF16))
        sgst1 = _es7.enter_context(nc.sbuf_tensor("sgst1", [128, 2048], BF16))
        ps_ga = _es7.enter_context(nc.psum_tensor("ps_ga", [128, 4 * 512], F32))
        ps_gb = _es7.enter_context(nc.psum_tensor("ps_gb", [128, 4 * 512], F32))
        wgv = w1g[:].rearrange("p (k c) -> p k c", k=KD)
        stg, psg = [sgst0, sgst1], [ps_ga, ps_gb]
        qg0, te0, ac0, qs0 = N["qg"], N["te"], N["ac"], N["qs"]
        n_units = NT * 2

        @block.gpsimd
        def _(g):
            for half in range(2):
                if half == 1:
                    W(g, "te", te0 + NT)
                for k in range(KD):
                    I(g.dma_start(wgv[:, k],
                                  w_gate[k * 128:(k + 1) * 128,
                                         half * 2048:(half + 1) * 2048]), "w1l", 16)

        @block.tensor
        def _(t):
            for u in range(n_units):
                half, tt = divmod(u, NT)
                if tt == 0:
                    W(t, "w1l", 16 * KD * (half + 1))
                if u >= 2:
                    W(t, "ac", ac0 + u - 1)
                ps = psg[u % 2]
                for k in range(KD):
                    for w in range(4):
                        i = t.matmul(ps[:, w * 512:(w + 1) * 512],
                                     xv[:, k, (tt + 1) * 128:(tt + 2) * 128],
                                     wgv[:, k, w * 512:(w + 1) * 512],
                                     start=(k == 0), stop=(k == KD - 1))
                I(i, "te")

        @block.scalar
        def _(s):
            for u in range(n_units):
                W(s, "te", te0 + u + 1)
                if u >= 2:
                    W(s, "sga" if u % 2 == 0 else "sgb", 16 * ((u - 2) // 2 + 1))
                ps = psg[u % 2]
                for w in range(4):
                    i = s.activation(stg[u % 2][:, w * 512:(w + 1) * 512],
                                     ps[:, w * 512:(w + 1) * 512], AF.Silu)
                I(i, "ac")

        @block.sync
        def _(sy):
            for u in range(n_units):
                half, tt = divmod(u, NT)
                W(sy, "ac", ac0 + u + 1)
                I(sy.dma_start(sg_scr[tt, :, half * 2048:(half + 1) * 2048],
                               stg[u % 2][:]), "sga" if u % 2 == 0 else "sgb", 16)
            if taps:
                WALL(sy, "sga", "sgb")
                I(sy.dma_start(tp["t_sg"].rearrange("t p c -> p t c"),
                               sg_scr.rearrange("t p c -> p t c")), "tap", 16)

        _es7.close()
    if phases >= 3:
        phase_barrier()
        HBC_WINS = [(127 + i * 512, 512) for i in range(NW)] + [(127 + NW * 512, 3)]
        DT_WINS = [(128 + i * 512, 512) for i in range(NW)]
        NWH = len(HBC_WINS)
        _es8 = ExitStack()
        w1h0 = _es8.enter_context(nc.sbuf_tensor("w1h0", [128, KD * 128], BF16))
        w1h1 = _es8.enter_context(nc.sbuf_tensor("w1h1", [128, KD * 128], BF16))
        cin0 = _es8.enter_context(nc.sbuf_tensor("cin0", [128, T + 3], F32))
        cin1 = _es8.enter_context(nc.sbuf_tensor("cin1", [128, T + 3], F32))
        cp0 = _es8.enter_context(nc.sbuf_tensor("cp0", [128, T], F32))
        cp1 = _es8.enter_context(nc.sbuf_tensor("cp1", [128, T], F32))
        cp2 = _es8.enter_context(nc.sbuf_tensor("cp2", [128, T], F32))
        co0 = _es8.enter_context(nc.sbuf_tensor("co0", [128, T], BF16))
        co1 = _es8.enter_context(nc.sbuf_tensor("co1", [128, T], BF16))
        co2 = _es8.enter_context(nc.sbuf_tensor("co2", [128, T], BF16))
        convw_sb = _es8.enter_context(nc.sbuf_tensor("convw_sb", [128, NOC * 4], F32))
        convb_sb = _es8.enter_context(nc.sbuf_tensor("convb_sb", [128, NOC], F32))
        dtb_sb = _es8.enter_context(nc.sbuf_tensor("dtb_sb", [NH, 1], F32))
        nega_sb = _es8.enter_context(nc.sbuf_tensor("nega_sb", [NH, 1], F32))
        dt_f = _es8.enter_context(nc.sbuf_tensor("dt_f", [NH, T], F32))
        abt = _es8.enter_context(nc.sbuf_tensor("abt", [NH, T], F32))
        acum = _es8.enter_context(nc.sbuf_tensor("acum", [NH, T], F32))
        acp = _es8.enter_context(nc.sbuf_tensor("acp", [NH, T], F32))
        tot = _es8.enter_context(nc.sbuf_tensor("tot", [NH, NCH], F32))
        suma_c = _es8.enter_context(nc.sbuf_tensor("suma_c", [NH, 1], F32))
        tott_sb = _es8.enter_context(nc.sbuf_tensor("tott_sb", [NCH, NH], F32))
        ps_ring = _es8.enter_context(nc.psum_tensor("ps_ring", [128, 7 * 512], F32))
        ps_trf = _es8.enter_context(nc.psum_tensor("ps_trf", [128, 384], F32))
        w1hv = [b[:].rearrange("p (k c) -> p k c", k=KD) for b in (w1h0, w1h1)]
        cwv = convw_sb[:].rearrange("p (j c) -> p j c", j=NOC)
        cins = [cin0, cin1]
        couts = [co0, co1, co2]
        TOT_OC = NOC + 1
        qg0, qs0, te0, dv0, ac0 = N["qg"], N["qs"], N["te"], N["dv"], N["ac"]
        te_mm = lambda oc: te0 + oc + 1
        ac_nega = ac0 + 2
        ac_silu = lambda oc: ac_nega + oc + 1

        def dv_evac(oc, wi):
            if oc < NOC:
                return dv0 + oc * (NWH + 2) + wi + 1
            return dv0 + NOC * (NWH + 2) + wi + 1

        dv_conv = lambda oc: dv0 + oc * (NWH + 2) + NWH + 2
        dv_p1b_evac_end = dv0 + NOC * (NWH + 2) + NW
        sp_st = lambda oc: "sp%d" % (oc % 3)

        def win_seq(oc, wi):
            return (oc * NWH + wi) if oc < NOC else (NOC * NWH + wi)

        @block.gpsimd
        def _(g):
            I(g.dma_start(convw_sb[:].rearrange("p (j c) -> p j c", j=NOC),
                          conv_wt.rearrange("(j p) c -> p j c", p=128)), "cw", 16)
            I(g.dma_start(convb_sb[:].rearrange("p (j o) -> p j o", o=1),
                          conv_bc.rearrange("(j p) o -> p j o", p=128),
                          allow_slow_non_contiguous=True), "cw", 16)
            I(g.dma_start(dtb_sb[:], dt_bias), "cw", 16)
            I(g.dma_start(nega_sb[:], a_log), "cw", 16)
            for oc in range(TOT_OC):
                if oc >= 2:
                    W(g, "te", te_mm(oc - 2))
                if oc < NOC:
                    I(g.dma_start(w1hv[oc % 2][:, :, :],
                                  w_hbc.rearrange("(k p) c -> p k c", p=128)
                                  [:, :, oc * 128:(oc + 1) * 128]),
                      "wha" if oc % 2 == 0 else "whb", 16)
                else:
                    I(g.dma_start(w1hv[oc % 2][:, :, 0:NH],
                                  w_dt.rearrange("(k p) c -> p k c", p=128)),
                      "wha" if oc % 2 == 0 else "whb", 16)

        @block.scalar
        def _(s):
            W(s, "cw", 16 * 4)
            I(s.activation(nega_sb[:], nega_sb[:], AF.Exp), "ac")
            W(s, "ac", N["ac"])
            I(s.mul(nega_sb[:], nega_sb[:], -1.0), "ac")
            for oc in range(NOC):
                W(s, "dv", dv_conv(oc))
                I(s.activation(couts[oc % 3][:], [cp1, cp2][oc % 2][:], AF.Silu), "ac")
            assert N["ac"] == ac_silu(NOC - 1)

        @block.tensor
        def _(t):
            for oc in range(TOT_OC):
                W(t, "wha" if oc % 2 == 0 else "whb", 16 * (oc // 2 + 1))
                wins = HBC_WINS if oc < NOC else DT_WINS
                prow = 128 if oc < NOC else NH
                for wi in range(len(wins)):
                    sq = win_seq(oc, wi)
                    if sq >= 7:
                        poc, pwi = divmod(sq - 7, NWH) if sq - 7 < NOC * NWH \
                            else (NOC, sq - 7 - NOC * NWH)
                        W(t, "dv", dv_evac(poc, pwi))
                for k in range(KD):
                    for wi, (c0, wd) in enumerate(wins):
                        sl = win_seq(oc, wi) % 7
                        i = t.matmul(ps_ring[0:prow, sl * 512:sl * 512 + wd],
                                     w1hv[oc % 2][:, k, 0:prow],
                                     xv[:, k, c0:c0 + wd],
                                     start=(k == 0), stop=(k == KD - 1))
                I(i, "te")
                assert N["te"] == te_mm(oc)

        @block.vector
        def _(v):
            W(v, "cw", 16 * 4)
            for oc in range(TOT_OC):
                wins = HBC_WINS if oc < NOC else DT_WINS
                prow = 128 if oc < NOC else NH
                W(v, "te", te_mm(oc))
                coff = wins[0][0]
                for wi, (c0, wd) in enumerate(wins):
                    sl = win_seq(oc, wi) % 7
                    if oc < NOC:
                        dst = cins[oc % 2][:, c0 - coff:c0 - coff + wd]
                    else:
                        dst = dt_f[0:prow, c0 - 128:c0 - 128 + wd]
                    I(v.tensor_copy(dst, ps_ring[0:prow, sl * 512:sl * 512 + wd]), "dv")
                    assert N["dv"] == dv_evac(oc, wi), ("EV", N["dv"], dv_evac(oc, wi), oc, wi)
                if oc < NOC:
                    if oc >= 3:
                        W(v, sp_st(oc - 3), 16 * ((oc - 3) // 3 + 1))
                    if oc >= 2:
                        W(v, "ac", ac_silu(oc - 2))
                    cb = cins[oc % 2]
                    cpe = [cp1, cp2][oc % 2]
                    v.tensor_scalar(cp0[:], cb[:, 0:T], cwv[:, oc, 0:1],
                                    convb_sb[:, oc:oc + 1], ALU.mult, ALU.add)
                    v.scalar_tensor_tensor(cpe[:], cb[:, 1:T + 1], cwv[:, oc, 1:2], cp0[:],
                                           ALU.mult, ALU.add)
                    v.scalar_tensor_tensor(cp0[:], cb[:, 2:T + 2], cwv[:, oc, 2:3], cpe[:],
                                           ALU.mult, ALU.add)
                    I(v.scalar_tensor_tensor(cpe[:], cb[:, 3:T + 3], cwv[:, oc, 3:4], cp0[:],
                                             ALU.mult, ALU.add), "dv")
                    W(v, "dv", N["dv"])
                    I(v.engine_nop(), "dv")
                    assert N["dv"] == dv_conv(oc), (N["dv"], dv_conv(oc), oc)
            assert N["dv"] == dv_p1b_evac_end


        @block.sync
        def _(sy):
            for oc in range(TOT_OC - 1):
                W(sy, "ac", ac_silu(oc))
                if oc < NHS:
                    I(sy.dma_start(hsT_scr[oc], couts[oc % 3][:]), sp_st(oc), 16)
                else:
                    I(sy.dma_start(bc_scr[oc - NHS], couts[oc % 3][:]), sp_st(oc), 16)
            if taps:
                I(sy.dma_start(tp["t_bc"].rearrange("j p t -> p j t"),
                               bc_scr.rearrange("j p t -> p j t")), "tap", 16)

        dv1 = N["dv"]

        @block.vector
        def _(v):
            I(v.tensor_scalar(dt_f[:], dt_f[:], dtb_sb[:], None, ALU.add), "dv")
            W(v, "dv", N["dv"])
            I(v.engine_nop(), "dv")

        @block.scalar
        def _(s):
            W(s, "dv", dv1 + 2)
            s.activation(abt[:], dt_f[:], AF.Abs)
            s.activation(abt[:], abt[:], AF.Exp, scale=-1.0)
            I(s.activation(abt[:], abt[:], AF.Ln, bias=1.0), "ac")
        ac_sp = N["ac"]

        @block.vector
        def _(v):
            W(v, "ac", ac_sp)
            v.tensor_scalar(dt_f[:], dt_f[:], 0.0, None, ALU.max)
            v.tensor_tensor(dt_f[:], dt_f[:], abt[:], ALU.add)
            v.tensor_scalar(abt[:], dt_f[:], nega_sb[:], None, ALU.mult)
            v.tensor_reduce(tot[:], abt[:].rearrange("p (c l) -> p c l", c=NCH),
                            AX.X, ALU.add)
            cur, nxt = abt, acum
            for i_ in range(7):
                sft = 1 << i_
                cvv = cur[:].rearrange("p (c l) -> p c l", c=NCH)
                nvv = nxt[:].rearrange("p (c l) -> p c l", c=NCH)
                v.tensor_copy(nvv[:, :, 0:sft], cvv[:, :, 0:sft])
                v.tensor_tensor(nvv[:, :, sft:128], cvv[:, :, sft:128],
                                cvv[:, :, 0:128 - sft], ALU.add)
                cur, nxt = nxt, cur
            assert cur is acum
            v.reduce_sum(suma_c[:], tot[:], AX.X)
            totb, _ = bass.broadcast_tensor_aps(
                tot[:].rearrange("p (c o) -> p c o", o=1),
                acum[:].rearrange("p (c l) -> p c l", c=NCH))
            I(v.tensor_tensor(acp[:].rearrange("p (c l) -> p c l", c=NCH), totb,
                              acum[:].rearrange("p (c l) -> p c l", c=NCH),
                              ALU.subtract), "dv")
            W(v, "dv", N["dv"])
            I(v.engine_nop(), "dv")
        dv_apl = N["dv"]

        te1 = N["te"]

        @block.tensor
        def _(t):
            W(t, "dv", dv_apl)
            for c in range(NCH):
                if c >= 1:
                    W(t, "dv", dv_apl + 3 * c)
                I(t.transpose(ps_trf[:, 0:NH], acum[0:NH, c * 128:(c + 1) * 128], id_f32[0:NH, 0:NH]), "te")
                I(t.transpose(ps_trf[:, 128:128 + NH], acp[0:NH, c * 128:(c + 1) * 128], id_f32[0:NH, 0:NH]), "te")
                I(t.transpose(ps_trf[:, 256:256 + NH], dt_f[0:NH, c * 128:(c + 1) * 128], id_f32[0:NH, 0:NH]), "te")
            W(t, "dv", dv_apl + 3 * NCH)
            I(t.transpose(ps_trf[0:NCH, 0:NH], tot[0:NH, 0:NCH], id_f32[0:NH, 0:NH]), "te")
            I(t.transpose(ps_trf[0:1, 128:128 + NH], suma_c[0:NH, 0:1], id_f32[0:NH, 0:NH]), "te")

        @block.vector
        def _(v):
            for c in range(NCH):
                W(v, "te", te1 + 3 * c + 1)
                I(v.tensor_copy(acumT[:, c * NH:(c + 1) * NH], ps_trf[0:128, 0:NH]), "dv")
                W(v, "te", te1 + 3 * c + 2)
                I(v.tensor_copy(decayT[:, c * NH:(c + 1) * NH], ps_trf[0:128, 128:128 + NH]), "dv")
                W(v, "te", te1 + 3 * c + 3)
                I(v.tensor_copy(dtT[:, c * NH:(c + 1) * NH], ps_trf[0:128, 256:256 + NH]), "dv")
            W(v, "te", te1 + 3 * NCH + 2)
            v.tensor_copy(tott_sb[:], ps_trf[0:NCH, 0:NH])
            I(v.tensor_copy(suma_row[:], ps_trf[0:1, 128:128 + NH]), "dv")
            W(v, "dv", N["dv"])
            I(v.engine_nop(), "dv")
        dv_atr = N["dv"]

        @block.scalar
        def _(s):
            W(s, "dv", dv_atr)
            s.activation(e_chunk[:, 0:NCH * NH], acumT[:, 0:NCH * NH], AF.Exp)
            I(s.activation(decayT[:, 0:NCH * NH], decayT[:, 0:NCH * NH], AF.Exp), "ac")

        @block.vector
        def _(v):
            W(v, "ac", N["ac"])
            I(v.tensor_tensor(decayT[:], decayT[:], dtT[:], ALU.mult), "dv")
            W(v, "dv", N["dv"])
            I(v.engine_nop(), "dv")

        @block.sync
        def _(sy):
            W(sy, "dv", dv_atr)
            I(sy.dma_start(acum_scr, acum[:]), "asp", 16)
            I(sy.dma_start(tott_scr, tott_sb[:]), "asp", 16)
            if taps:
                I(sy.dma_start(tp["t_acum"], acum[:]), "tap", 16)

        @block.gpsimd
        def _(g):
            W(g, "asp", N["asp"])
            I(g.dma_start(esc_rep[:].rearrange("p (c h) -> p c h", c=NCH),
                          bass.AP(tott_scr.tensor, 0, [[0, 128], [NH, NCH], [1, NH]])), "erl", 16)

        @block.scalar
        def _(s):
            W(s, "erl", N["erl"])
            I(s.activation(esc_rep[:], esc_rep[:], AF.Exp), "ac")
        if taps:
            @block.sync
            def _(sy):
                W(sy, "ac", N["ac"])
                W(sy, "dv", N["dv"])
                I(sy.dma_start(tp["t_dec"], decayT[:]), "tap", 16)
                I(sy.dma_start(tp["t_esc"], esc_rep[:]), "tap", 16)

        _es8.close()
    _esx.close()
    if phases >= 4:
        phase_barrier()
        _es9 = ExitStack()
        hld0 = _es9.enter_context(nc.sbuf_tensor("hld0", [128, T], BF16))
        hld1 = _es9.enter_context(nc.sbuf_tensor("hld1", [128, T], BF16))
        hst0 = _es9.enter_context(nc.sbuf_tensor("hst0", [128, NT * 128], BF16))
        hst1 = _es9.enter_context(nc.sbuf_tensor("hst1", [128, NT * 128], BF16))
        ps_h0 = _es9.enter_context(nc.psum_tensor("ps_h0", [128, 128], BF16))
        ps_h1 = _es9.enter_context(nc.psum_tensor("ps_h1", [128, 128], BF16))
        hld, hst, psh = [hld0, hld1], [hst0, hst1], [ps_h0, ps_h1]
        qs0, te0, dv0 = N["qs"], N["te"], N["dv"]
        dv_ev = lambda oc, tt: dv0 + oc * (NT + 1) + tt + 1
        dv_oc_end = lambda oc: dv0 + (oc + 1) * (NT + 1)
        hl_st = lambda oc: "hla" if oc % 2 == 0 else "hlb"
        hs_st = lambda oc: "hsa" if oc % 2 == 0 else "hsb"

        @block.sync
        def _(sy):
            for oc in range(NHS):
                if oc >= 2:
                    W(sy, "dv", dv_oc_end(oc - 2))
                I(sy.dma_start(hld[oc % 2][:], hsT_scr[oc]), hl_st(oc), 16)
                if oc >= 1:
                    W(sy, "dv", dv_oc_end(oc - 1))
                    I(sy.dma_start(hs_scr.rearrange("t j p l -> p t j l")[:, :, oc - 1, :],
                                   hst[(oc - 1) % 2][:].rearrange("p (t l) -> p t l", t=NT)),
                      hs_st(oc - 1), 16)
            W(sy, "dv", dv_oc_end(NHS - 1))
            I(sy.dma_start(hs_scr.rearrange("t j p l -> p t j l")[:, :, NHS - 1, :],
                           hst[(NHS - 1) % 2][:].rearrange("p (t l) -> p t l", t=NT)),
              hs_st(NHS - 1), 16)
            if taps:
                I(sy.dma_start(tp["t_hs"].rearrange("t j p l -> p t j l"),
                               hs_scr.rearrange("t j p l -> p t j l")), "tap", 16)

        @block.tensor
        def _(t):
            for oc in range(NHS):
                W(t, hl_st(oc), 16 * (oc // 2 + 1))
                for tt in range(NT):
                    gsl = oc * NT + tt
                    if gsl >= 2:
                        poc, ptt = divmod(gsl - 2, NT)
                        W(t, "dv", dv_ev(poc, ptt) + 1)
                    I(t.transpose(psh[gsl % 2][:], hld[oc % 2][:, tt * 128:(tt + 1) * 128],
                                  id_bf[:]), "te")

        @block.vector
        def _(v):
            for oc in range(NHS):
                if oc >= 2:
                    W(v, hs_st(oc - 2), 16 * ((oc - 2) // 2 + 1))
                for tt in range(NT):
                    gsl = oc * NT + tt
                    W(v, "te", te0 + gsl + 1)
                    I(v.tensor_copy(hst[oc % 2][:, tt * 128:(tt + 1) * 128],
                                    psh[gsl % 2][:]), "dv")
                W(v, "dv", N["dv"])
                I(v.engine_nop(), "dv")

        _es9.close()
    if phases >= 5:
        phase_barrier()
        _es10 = ExitStack()
        hstm0 = _es10.enter_context(nc.sbuf_tensor("hstm0", [128, INTER], BF16))
        hstm1 = _es10.enter_context(nc.sbuf_tensor("hstm1", [128, INTER], BF16))
        btc0 = _es10.enter_context(nc.sbuf_tensor("btc0", [128, 16 * 128], BF16))
        btc1 = _es10.enter_context(nc.sbuf_tensor("btc1", [128, 16 * 128], BF16))
        btm = _es10.enter_context(nc.sbuf_tensor("btm", [128, 128], BF16))
        bdec = _es10.enter_context(nc.sbuf_tensor("bdec", [128, 8 * 128], BF16))
        tmp512 = _es10.enter_context(nc.sbuf_tensor("tmp512", [128, 512], F32))
        ps_sc0 = _es10.enter_context(nc.psum_tensor("ps_sc0", [128, 512], F32))
        ps_sc1 = _es10.enter_context(nc.psum_tensor("ps_sc1", [128, 512], F32))
        ps_tb = _es10.enter_context(nc.psum_tensor("ps_tb", [128, 128], BF16))
        hstm, btc, pssc = [hstm0, hstm1], [btc0, btc1], [ps_sc0, ps_sc1]
        qs0, te0, dv0 = N["qs"], N["te"], N["dv"]
        dv_grp = lambda gi, j: dv0 + 1 + gi * 4 + j
        te_grp = lambda gi, j: te0 + gi * 2 + j

        @block.vector
        def _(v):
            W(v, "ac", N["ac"])
            I(v.memset(s_state[:], 0.0), "dv")

        @block.sync
        def _(sy):
            for c in range(NCH):
                if c >= 2:
                    W(sy, "te", te_grp((c - 1) * NG - 1, 2))
                st = "c2a" if c % 2 == 0 else "c2b"
                I(sy.dma_start(hstm[c % 2][:].rearrange("p (j l) -> p j l", j=NHS),
                               hs_scr.rearrange("t j p l -> p t j l")[:, c]), st, 16)
                I(sy.dma_start(btc[c % 2][:].rearrange("p (j l) -> p j l", j=16),
                               bc_scr.rearrange("j p t -> p j t")[:, :, c * 128:(c + 1) * 128]),
                  st, 16)

        @block.tensor
        def _(t):
            for c in range(NCH):
                W(t, "c2a" if c % 2 == 0 else "c2b", 32 * (c // 2 + 1))
                for g_ in range(NG):
                    gi = c * NG + g_
                    if gi >= 1:
                        W(t, "dv", dv_grp(gi - 1, 2))
                    I(t.transpose(ps_tb[:], btc[c % 2][:, g_ * 128:(g_ + 1) * 128], id_bf[:]), "te")
                    W(t, "dv", dv_grp(gi, 2))
                    if gi >= 2:
                        W(t, "dv", dv_grp(gi - 1, 4))
                    ps = pssc[gi % 2]
                    for h in range(GS):
                        i = t.matmul(ps[:, h * HD:(h + 1) * HD],
                                     bdec[:, h * 128:(h + 1) * 128],
                                     hstm[c % 2][:, (g_ * GS + h) * HD:(g_ * GS + h + 1) * HD],
                                     start=True, stop=True)
                    I(i, "te")

        @block.vector
        def _(v):
            for c in range(NCH):
                for g_ in range(NG):
                    gi = c * NG + g_
                    W(v, "te", te_grp(gi, 1))
                    for h in range(GS):
                        hh = c * NH + g_ * GS + h
                        i = v.tensor_scalar(bdec[:, h * 128:(h + 1) * 128], ps_tb[:],
                                            decayT[:, hh:hh + 1], None, ALU.mult)
                    I(i, "dv")
                    W(v, "dv", N["dv"])
                    I(v.engine_nop(), "dv")
                    W(v, "te", te_grp(gi, 2))
                    sl = s_state[:, g_ * 512:(g_ + 1) * 512]
                    escb, _ = bass.broadcast_tensor_aps(
                        esc_rep[:, c * NH + g_ * GS:c * NH + (g_ + 1) * GS]
                        .rearrange("p (h o) -> p h o", o=1),
                        tmp512[:].rearrange("p (h d) -> p h d", h=GS))
                    v.tensor_tensor(tmp512[:].rearrange("p (h d) -> p h d", h=GS),
                                    sl.rearrange("p (h d) -> p h d", h=GS), escb, ALU.mult)
                    I(v.tensor_tensor(sl, tmp512[:], pssc[gi % 2][:], ALU.add), "dv")
                    W(v, "dv", N["dv"])
                    I(v.engine_nop(), "dv")

        @block.gpsimd
        def _(g):
            W(g, "dv", N["dv"])
            I(g.dma_start(agin.ap()[0:128, :], s_state[:]), "agi", 16)
            I(g.dma_start(agin.ap()[128:129, 0:NH], suma_row[:]), "agi", 16)
            W(g, "agi", N["agi"])
            I(g.collective_compute("AllGather", ALU.bypass,
                                   replica_groups=[list(range(8))],
                                   ins=[agin.ap().opt()], outs=[agout.ap().opt()]), "cc")
        if taps:
            @block.sync
            def _(sy):
                W(sy, "dv", N["dv"])
                I(sy.dma_start(tp["t_sloc"], s_state[:]), "tap", 16)

        _es10.close()
    if phases >= 6:
        phase_barrier()
        _es11 = ExitStack()
        d_rep = _es11.enter_context(nc.sbuf_tensor("d_rep", [128, INTER], F32))
        hstm0b = _es11.enter_context(nc.sbuf_tensor("hstm0b", [128, INTER], BF16))
        hstm1b = _es11.enter_context(nc.sbuf_tensor("hstm1b", [128, INTER], BF16))
        btcb0 = _es11.enter_context(nc.sbuf_tensor("btcb0", [128, 16 * 128], BF16))
        btcb1 = _es11.enter_context(nc.sbuf_tensor("btcb1", [128, 16 * 128], BF16))
        sgc0 = _es11.enter_context(nc.sbuf_tensor("sgc0", [128, INTER], BF16))
        sgc1 = _es11.enter_context(nc.sbuf_tensor("sgc1", [128, INTER], BF16))
        btmb = _es11.enter_context(nc.sbuf_tensor("btmb", [128, 128], BF16))
        bdecb = _es11.enter_context(nc.sbuf_tensor("bdecb", [128, 8 * 128], BF16))
        s_bf = _es11.enter_context(nc.sbuf_tensor("s_bf", [128, NH * HD], BF16))
        rg0 = _es11.enter_context(nc.sbuf_tensor("rg0", [128, 8 * 128], F32))
        rg1 = _es11.enter_context(nc.sbuf_tensor("rg1", [128, 8 * 128], F32))
        ltg = _es11.enter_context(nc.sbuf_tensor("ltg", [128, 8 * 128], F32))
        cbtm = _es11.enter_context(nc.sbuf_tensor("cbtm", [128, 128], F32))
        mtg = _es11.enter_context(nc.sbuf_tensor("mtg", [128, 8 * 128], BF16))
        ysc = _es11.enter_context(nc.sbuf_tensor("ysc", [128, INTER], F32))
        tmp4k = _es11.enter_context(nc.sbuf_tensor("tmp4k", [128, INTER], F32))
        yos = _es11.enter_context(nc.sbuf_tensor("yos", [128, 512], F32))
        yn_bf = _es11.enter_context(nc.sbuf_tensor("yn_bf", [128, INTER], BF16))
        yntst = _es11.enter_context(nc.sbuf_tensor("yntst", [128, INTER], BF16))
        rms_sc = _es11.enter_context(nc.sbuf_tensor("rms_sc", [128, 4], F32))
        sq_tr2 = _es11.enter_context(nc.sbuf_tensor("sq_tr2", [128, INTER], BF16))
        wfac = _es11.enter_context(nc.sbuf_tensor("wfac", [8, NH], F32))
        wfb = _es11.enter_context(nc.sbuf_tensor("wfb", [128, NH], F32))
        ebuf = _es11.enter_context(nc.sbuf_tensor("ebuf", [128, NH * HD], F32))
        suma_all = _es11.enter_context(nc.sbuf_tensor("suma_all", [8, NH], F32))
        tmT_sb = _es11.enter_context(nc.sbuf_tensor("tmT_sb", [8, 8], F32))
        incl_sb = _es11.enter_context(nc.sbuf_tensor("incl_sb", [8, 1], F32))
        ps_cb0 = _es11.enter_context(nc.psum_tensor("ps_cb0", [128, 128], F32))
        ps_cb1 = _es11.enter_context(nc.psum_tensor("ps_cb1", [128, 128], F32))
        ps_yd = _es11.enter_context(nc.psum_tensor("ps_yd", [128, 512], F32))
        ps_yo = _es11.enter_context(nc.psum_tensor("ps_yo", [128, 512], F32))
        ps_sc = _es11.enter_context(nc.psum_tensor("ps_sc", [128, 512], F32))
        ps_tbb = _es11.enter_context(nc.psum_tensor("ps_tbb", [128, 128], BF16))
        ps_yt0 = _es11.enter_context(nc.psum_tensor("ps_yt0", [128, 128], BF16))
        ps_yt1 = _es11.enter_context(nc.psum_tensor("ps_yt1", [128, 128], BF16))
        hstmb, btcb, sgc = [hstm0b, hstm1b], [btcb0, btcb1], [sgc0, sgc1]
        rgb = [rg0, rg1]
        psyt = [ps_yt0, ps_yt1]
        pscb = [ps_cb0, ps_cb1]

        @block.gpsimd
        def _(g):
            I(g.dma_start(d_rep[:], bass.AP(d_row.tensor, 0, [[0, 128], [1, INTER]])), "drl", 16)

        @block.gpsimd
        def _(g):
            W(g, "cc", N["cc"])
            I(g.dma_start(suma_all[:],
                          bass.AP(agout, 128 * NH * HD,
                                  [[129 * NH * HD, 8], [1, NH]])), "agc", 16)
            I(g.dma_start(tmT_sb[:], tm_T), "agc", 16)
            I(g.dma_start(incl_sb[:], incl_col), "agc", 16)
        te_ag = N["te"]

        @block.tensor
        def _(t):
            W(t, "agc", 48)
            W(t, "dv", N["dv"])
            I(t.matmul(ps_yd[0:8, 0:NH], tmT_sb[:], suma_all[:], start=True, stop=True), "te")

        @block.scalar
        def _(s):
            W(s, "te", te_ag + 1)
            I(s.activation(wfac[:], ps_yd[0:8, 0:NH], AF.Exp), "ac")
        ac_wf = N["ac"]

        @block.vector
        def _(v):
            W(v, "ac", ac_wf)
            I(v.tensor_scalar(wfac[:], wfac[:], incl_sb[:], None, ALU.mult), "dv")
            W(v, "dv", N["dv"])
            I(v.memset(s_state[:], 0.0), "dv")
        dv_wf = N["dv"]

        @block.sync
        def _(sy):
            W(sy, "dv", dv_wf)
            I(sy.dma_start(wfac_scr, wfac[:]), "wfs", 16)

        @block.gpsimd
        def _(g):
            W(g, "wfs", 16)
            for r in range(8):
                if r >= 1:
                    W(g, "dv", dv_wf + r)
                I(g.dma_start(wfb[:], bass.AP(wfac_scr.tensor, r * NH,
                                              [[0, 128], [1, NH]])), "rkl", 16)
                I(g.dma_start(ebuf[:], agout.ap()[r * 129:r * 129 + 128, :]), "rkl", 16)

        @block.vector
        def _(v):
            for r in range(8):
                W(v, "rkl", 32 * (r + 1))
                wfbb, _ = bass.broadcast_tensor_aps(
                    wfb[:].rearrange("p (h o) -> p h o", o=1),
                    tmp4k[:].rearrange("p (h d) -> p h d", h=NH))
                v.tensor_tensor(tmp4k[:].rearrange("p (h d) -> p h d", h=NH),
                                ebuf[:].rearrange("p (h d) -> p h d", h=NH), wfbb, ALU.mult)
                I(v.tensor_tensor(s_state[:], s_state[:], tmp4k[:], ALU.add), "dv")
        if taps:
            @block.sync
            def _(sy):
                W(sy, "dv", N["dv"])
                I(sy.dma_start(tp["t_sglob"], s_state[:]), "tap", 16)

        qs0, qg0c, te0c, dv0c, ac0c = N["qs"], N["qg"], N["te"], N["dv"], N["ac"]
        DVG = 12
        DVC = NG * DVG + 2 + 5 + NHS + 1
        TEC = 5 * NG + NHS
        ACC_ = 2 * NG + 4
        dv_sbf = lambda c: dv0c + c * DVC + 2
        dv_g = lambda c, g_, j: dv0c + c * DVC + 2 + g_ * DVG + j
        dv_epi = lambda c, j: dv0c + c * DVC + 2 + NG * DVG + j
        dv_ynt = lambda c, jj: dv0c + c * DVC + 2 + NG * DVG + 5 + jj + 1
        dv_cend = lambda c: dv0c + (c + 1) * DVC
        te_g = lambda c, g_, j: te0c + c * TEC + g_ * 5 + j
        te_ynt = lambda c, jj: te0c + c * TEC + 5 * NG + jj + 1
        ac_lt = lambda c, g_: ac0c + c * ACC_ + 2 * (g_ + 1)
        ac_rms = lambda c: ac0c + c * ACC_ + 2 * NG + 4
        ch_st = lambda c: "cha" if c % 2 == 0 else "chb"
        rg_st = lambda gi: "rga" if gi % 2 == 0 else "rgb"

        @block.sync
        def _(sy):
            def spill(c):
                W(sy, "dv", dv_cend(c))
                I(sy.dma_start(ynt_scr.rearrange("c j p l -> p c j l")[:, c],
                               yntst[:].rearrange("p (j l) -> p j l", j=NHS)), "ysp", 16)
                if taps:
                    I(sy.dma_start(tp["t_y"].rearrange("c p f -> p c f")[:, c], ysc[:]),
                      "tap", 16)
            for c in range(NCH):
                if c >= 2:
                    W(sy, "dv", dv_cend(c - 1))
                I(sy.dma_start(hstmb[c % 2][:].rearrange("p (j l) -> p j l", j=NHS),
                               hs_scr.rearrange("t j p l -> p t j l")[:, c]), ch_st(c), 16)
                I(sy.dma_start(btcb[c % 2][:].rearrange("p (j l) -> p j l", j=16),
                               bc_scr.rearrange("j p t -> p j t")[:, :, c * 128:(c + 1) * 128]),
                  ch_st(c), 16)
                I(sy.dma_start(sgc[c % 2][:], sg_scr[c]), ch_st(c), 16)
                if c >= 1:
                    spill(c - 1)
            spill(NCH - 1)

        @block.gpsimd
        def _(g):
            for c in range(NCH):
                for g_ in range(NG):
                    gi = c * NG + g_
                    if gi >= 2:
                        pc, pg = divmod(gi - 2, NG)
                        W(g, "dv", dv_g(pc, pg, 6))
                    I(g.dma_start(rgb[gi % 2][:].rearrange("p (h l) -> p h l", h=GS),
                                  bass.AP(acum_scr.tensor, (g_ * GS) * T + c * 128,
                                          [[0, 128], [T, GS], [1, 128]])), rg_st(gi), 16)

        @block.tensor
        def _(t):
            for c in range(NCH):
                W(t, ch_st(c), 48 * (c // 2 + 1))
                for g_ in range(NG):
                    gi = c * NG + g_
                    if gi >= 2:
                        pc, pg = divmod(gi - 2, NG)
                        W(t, "dv", dv_g(pc, pg, 2))
                    I(t.matmul(pscb[g_ % 2][:],
                               btcb[c % 2][:, g_ * 128:(g_ + 1) * 128],
                               btcb[c % 2][:, (8 + g_) * 128:(9 + g_) * 128],
                               start=True, stop=True), "te")
                    if gi >= 1:
                        pc, pg = divmod(gi - 1, NG)
                        W(t, "dv", dv_g(pc, pg, 4))
                    I(t.transpose(ps_tbb[:], btcb[c % 2][:, g_ * 128:(g_ + 1) * 128],
                                  id_bf[:]), "te")
                    W(t, "dv", dv_g(c, g_, 8))
                    for h in range(GS):
                        i = t.matmul(ps_yd[:, h * HD:(h + 1) * HD],
                                     mtg[:, h * 128:(h + 1) * 128],
                                     hstmb[c % 2][:, (g_ * GS + h) * HD:(g_ * GS + h + 1) * HD],
                                     start=True, stop=True)
                    I(i, "te")
                    W(t, "dv", dv_g(c, g_, 4))
                    for h in range(GS):
                        i = t.matmul(ps_sc[:, h * HD:(h + 1) * HD],
                                     bdecb[:, h * 128:(h + 1) * 128],
                                     hstmb[c % 2][:, (g_ * GS + h) * HD:(g_ * GS + h + 1) * HD],
                                     start=True, stop=True)
                    I(i, "te")
                    for h in range(GS):
                        i = t.matmul(ps_yo[:, h * HD:(h + 1) * HD],
                                     btcb[c % 2][:, (8 + g_) * 128:(9 + g_) * 128],
                                     s_bf[:, (g_ * GS + h) * HD:(g_ * GS + h + 1) * HD],
                                     start=True, stop=True)
                    I(i, "te")
                W(t, "dv", dv_epi(c, 5))
                for jj in range(NHS):
                    gsl = c * NHS + jj
                    if gsl >= 2:
                        pc, pj = divmod(gsl - 2, NHS)
                        W(t, "dv", dv_ynt(pc, pj))
                    I(t.transpose(psyt[gsl % 2][:], yn_bf[:, jj * 128:(jj + 1) * 128],
                                  id_bf[:]), "te")

        @block.scalar
        def _(s):
            for c in range(NCH):
                for g_ in range(NG):
                    W(s, "dv", dv_g(c, g_, 6))
                    I(s.activation(ltg[:], ltg[:], AF.Exp), "ac")
                    W(s, "ac", N["ac"])
                    I(s.mul(rms_sc[:, 3:4], rms_sc[:, 3:4], 1.0), "ac")
                W(s, "dv", dv_epi(c, 3))
                I(s.activation(rms_sc[:, 1:2], rms_sc[:, 0:1], AF.Identity,
                               scale=1.0 / INTER, bias=epsrm_c[:, 0:1]), "ac")
                W(s, "ac", N["ac"])
                I(s.activation(rms_sc[:, 2:3], rms_sc[:, 1:2], AF.Ln), "ac")
                W(s, "ac", N["ac"])
                I(s.activation(rms_sc[:, 2:3], rms_sc[:, 2:3], AF.Exp, scale=-0.5), "ac")
                W(s, "ac", N["ac"])
                I(s.mul(rms_sc[:, 3:4], rms_sc[:, 3:4], 1.0), "ac")

        @block.vector
        def _(v):
            W(v, "drl", 16)
            for c in range(NCH):
                I(v.tensor_copy(s_bf[:], s_state[:]), "dv")
                W(v, "dv", N["dv"])
                I(v.engine_nop(), "dv")
                assert N["dv"] == dv_sbf(c), (N["dv"], dv_sbf(c), c)
                for g_ in range(NG):
                    gi = c * NG + g_
                    W(v, "te", te_g(c, g_, 1))
                    I(v.tensor_tensor(cbtm[:], pscb[g_ % 2][:], mask_triu[:], ALU.mult), "dv")
                    W(v, "dv", N["dv"])
                    I(v.engine_nop(), "dv")
                    W(v, "te", te_g(c, g_, 2))
                    for h in range(GS):
                        hh = c * NH + g_ * GS + h
                        i = v.tensor_scalar(bdecb[:, h * 128:(h + 1) * 128], ps_tbb[:],
                                            decayT[:, hh:hh + 1], None, ALU.mult)
                    I(i, "dv")
                    W(v, "dv", N["dv"])
                    I(v.engine_nop(), "dv")
                    W(v, rg_st(gi), 16 * (gi // 2 + 1))
                    for h in range(GS):
                        hh = c * NH + g_ * GS + h
                        i = v.tensor_scalar(ltg[:, h * 128:(h + 1) * 128],
                                            rgb[gi % 2][:, h * 128:(h + 1) * 128],
                                            acumT[:, hh:hh + 1], 0.0,
                                            ALU.subtract, ALU.min)
                    I(i, "dv")
                    W(v, "dv", N["dv"])
                    I(v.engine_nop(), "dv")
                    W(v, "ac", ac_lt(c, g_))
                    cbb, _ = bass.broadcast_tensor_aps(
                        cbtm[:].rearrange("p (o l) -> p o l", o=1),
                        ltg[:].rearrange("p (h l) -> p h l", h=GS))
                    v.tensor_tensor(mtg[:].rearrange("p (h l) -> p h l", h=GS),
                                    ltg[:].rearrange("p (h l) -> p h l", h=GS),
                                    cbb, ALU.mult)
                    dtb, _ = bass.broadcast_tensor_aps(
                        dtT[:, c * NH + g_ * GS:c * NH + (g_ + 1) * GS]
                        .rearrange("p (h o) -> p h o", o=1),
                        mtg[:].rearrange("p (h l) -> p h l", h=GS))
                    I(v.tensor_tensor(mtg[:].rearrange("p (h l) -> p h l", h=GS),
                                      mtg[:].rearrange("p (h l) -> p h l", h=GS),
                                      dtb, ALU.mult), "dv")
                    W(v, "dv", N["dv"])
                    I(v.engine_nop(), "dv")
                    W(v, "te", te_g(c, g_, 5))
                    ecb, _ = bass.broadcast_tensor_aps(
                        e_chunk[:, c * NH + g_ * GS:c * NH + (g_ + 1) * GS]
                        .rearrange("p (h o) -> p h o", o=1),
                        yos[:].rearrange("p (h d) -> p h d", h=GS))
                    v.tensor_tensor(yos[:].rearrange("p (h d) -> p h d", h=GS),
                                    ps_yo[:].rearrange("p (h d) -> p h d", h=GS),
                                    ecb, ALU.mult)
                    I(v.tensor_tensor(ysc[:, g_ * 512:(g_ + 1) * 512], ps_yd[:], yos[:],
                                      ALU.add), "dv")
                    W(v, "dv", N["dv"])
                    I(v.engine_nop(), "dv")
                    escb, _ = bass.broadcast_tensor_aps(
                        esc_rep[:, c * NH + g_ * GS:c * NH + (g_ + 1) * GS]
                        .rearrange("p (h o) -> p h o", o=1),
                        tmp4k[:, 0:512].rearrange("p (h d) -> p h d", h=GS))
                    sl = s_state[:, g_ * 512:(g_ + 1) * 512]
                    v.tensor_tensor(tmp4k[:, 0:512].rearrange("p (h d) -> p h d", h=GS),
                                    sl.rearrange("p (h d) -> p h d", h=GS), escb, ALU.mult)
                    I(v.tensor_tensor(sl, tmp4k[:, 0:512], ps_sc[:], ALU.add), "dv")
                    W(v, "dv", N["dv"])
                    I(v.engine_nop(), "dv")
                    assert N["dv"] == dv_g(c, g_, 12), (N["dv"], dv_g(c, g_, 12))
                v.tensor_tensor(tmp4k[:], hstmb[c % 2][:], d_rep[:], ALU.mult)
                I(v.tensor_tensor(ysc[:], ysc[:], tmp4k[:], ALU.add), "dv")
                v.tensor_tensor(ysc[:], ysc[:], sgc[c % 2][:], ALU.mult)
                I(v.scalar_tensor_tensor(sq_tr2[:], ysc[:], 1.0, ysc[:], ALU.mult, ALU.mult,
                                         accum_out=rms_sc[:, 0:1]), "dv")
                W(v, "dv", N["dv"])
                I(v.engine_nop(), "dv")
                W(v, "ac", ac_rms(c))
                I(v.tensor_scalar(yn_bf[:], ysc[:], rms_sc[:, 2:3], None, ALU.mult), "dv")
                W(v, "dv", N["dv"])
                I(v.engine_nop(), "dv")
                assert N["dv"] == dv_epi(c, 5)
                if c >= 1:
                    W(v, "ysp", 16 * c)
                for jj in range(NHS):
                    W(v, "te", te_ynt(c, jj))
                    I(v.tensor_copy(yntst[:, jj * 128:(jj + 1) * 128],
                                    psyt[(c * NHS + jj) % 2][:]), "dv")
                W(v, "dv", N["dv"])
                I(v.engine_nop(), "dv")
                assert N["dv"] == dv_cend(c)



        _es11.close()

    _es2.close()
    if phases >= 7:
        phase_barrier()
        _es3 = ExitStack()
        w2h = _es3.enter_context(nc.sbuf_tensor("w2h", [128, NHS * 1024], BF16))
        ynt0 = _es3.enter_context(nc.sbuf_tensor("ynt0", [128, NHS * 128], BF16))
        ynt1 = _es3.enter_context(nc.sbuf_tensor("ynt1", [128, NHS * 128], BF16))
        ob0 = _es3.enter_context(nc.sbuf_tensor("ob0", [128, 1024], F32))
        ob1 = _es3.enter_context(nc.sbuf_tensor("ob1", [128, 1024], F32))
        ps_o0 = _es3.enter_context(nc.psum_tensor("ps_o0", [128, 1024], F32))
        ps_o1 = _es3.enter_context(nc.psum_tensor("ps_o1", [128, 1024], F32))
        yntb, obb, pso = [ynt0, ynt1], [ob0, ob1], [ps_o0, ps_o1]
        w2v = w2h[:].rearrange("p (j c) -> p j c", j=NHS)
        for ohalf in range(2):
            qg0, qs0, te0, dv0, ac0 = N["qg"], N["qs"], N["te"], N["dv"], N["ac"]
            te_mm = lambda tt: te0 + tt + 1
            ac_ev = lambda tt: ac0 + tt + 1
            qg_acc = lambda tt: qg0 + 16 * NHS + 16 * (tt + 1)

            @block.gpsimd
            def _(g):
                if ohalf == 1:
                    W(g, "te", te0)
                for j in range(NHS):
                    I(g.dma_start(w2v[:, j],
                                  w_out[j * 128:(j + 1) * 128,
                                        ohalf * 1024:(ohalf + 1) * 1024]), "w2l", 16)

            @block.vector
            def _(v):
                W(v, "w2l", N["w2l"])
                for j in range(NHS):
                    i = v.tensor_scalar(w2v[:, j], w2v[:, j], nw_sb[:, j:j + 1], None, ALU.mult)
                I(i, "dv")
                W(v, "dv", N["dv"])
                I(v.engine_nop(), "dv")
            dv_w2 = N["dv"]

            if taps:
                @block.sync
                def _(sy):
                    W(sy, "dv", dv_w2)
                    I(sy.dma_start(tp["t_w2h%d" % ohalf], w2h[:]), "tap", 16)

            yb_st = lambda tt: "yba" if tt % 2 == 0 else "ybb"
            a3_st = lambda u3: "a3a" if u3 % 2 == 0 else "a3b"

            @block.sync
            def _(sy):
                for tt in range(NT):
                    if tt >= 2:
                        W(sy, "te", te_mm(tt - 2))
                    elif ohalf == 1:
                        W(sy, "te", te0)
                    I(sy.dma_start(yntb[tt % 2][:].rearrange("p (j l) -> p j l", j=NHS),
                                   ynt_scr.rearrange("c j p l -> p c j l")[:, tt]),
                      yb_st(tt), 16)

            @block.tensor
            def _(t):
                W(t, "dv", dv_w2)
                if ohalf == 1:
                    W(t, "ac", ac0)
                for tt in range(NT):
                    W(t, yb_st(tt), 16 * ((ohalf * NT + tt) // 2 + 1))
                    if tt >= 2:
                        W(t, "ac", ac_ev(tt - 2))
                    ps = pso[tt % 2]
                    for j in range(NHS):
                        for w in range(2):
                            i = t.matmul(ps[:, w * 512:(w + 1) * 512],
                                         yntb[tt % 2][:, j * 128:(j + 1) * 128],
                                         w2v[:, j, w * 512:(w + 1) * 512],
                                         start=(j == 0), stop=(j == NHS - 1))
                    I(i, "te")

            @block.scalar
            def _(s):
                for tt in range(NT):
                    W(s, "te", te_mm(tt))
                    u3 = ohalf * NT + tt
                    if u3 >= 2:
                        W(s, a3_st(u3 - 2), 16 * ((u3 - 2) // 2 + 1))
                    I(s.activation(obb[tt % 2][:], pso[tt % 2][:], AF.Identity), "ac")

            @block.gpsimd
            def _(g):
                for tt in range(NT):
                    W(g, "ac", ac_ev(tt))
                    I(g.dma_start(acc_scr[tt * 128:(tt + 1) * 128,
                                          ohalf * 1024:(ohalf + 1) * 1024],
                                  obb[tt % 2][:]), a3_st(ohalf * NT + tt), 16)

        if taps:
            @block.sync
            def _(sy):
                WALL(sy, "a3a", "a3b")
                I(sy.dma_start(tp["t_acc3"], acc_scr), "tap", 16)
                I(sy.dma_start(tp["t_ynt"], ynt_scr), "tap", 16)

        _es3.close()
    if phases >= 8:
        phase_barrier()
        _es4 = ExitStack()
        xn4 = _es4.enter_context(nc.sbuf_tensor("xn4", [128, KD * T], BF16))
        he = _es4.enter_context(nc.sbuf_tensor("he", [128, NJ * T], BF16))
        wupb0 = _es4.enter_context(nc.sbuf_tensor("wupb0", [128, KD * 256], BF16))
        wupb1 = _es4.enter_context(nc.sbuf_tensor("wupb1", [128, KD * 256], BF16))
        wdnb = _es4.enter_context(nc.sbuf_tensor("wdnb", [128, NJ * D], BF16))
        sil = _es4.enter_context(nc.sbuf_tensor("sil", [128, T], F32))
        oe0 = _es4.enter_context(nc.sbuf_tensor("oe0", [128, D], F32))
        oe1 = _es4.enter_context(nc.sbuf_tensor("oe1", [128, D], F32))
        xn4v = xn4[:].rearrange("p (k t) -> p k t", k=KD)
        wup = [wupb0, wupb1]
        wupv = [b[:].rearrange("p (k c) -> p k c", k=KD) for b in wup]
        wdnv = wdnb[:].rearrange("p (j c) -> p j c", j=NJ)
        oeb = [oe0, oe1]

        @block.sync
        def _(sy):
            for k in range(KD):
                I(sy.dma_start(xn4v[:, k], xnt_scr[k]), "x4l", 16)

        _es12 = ExitStack()
        gwt = _es12.enter_context(nc.sbuf_tensor("gwt", [128, KD * 8], BF16))
        sc8 = _es12.enter_context(nc.sbuf_tensor("sc8", [8, T], F32))
        sctm = _es12.enter_context(nc.sbuf_tensor("sctm", [128, NT * 8], F32))
        sb8 = _es12.enter_context(nc.sbuf_tensor("sb8", [128, NT * 8], F32))
        srt_w = _es12.enter_context(nc.sbuf_tensor("srt_w", [128, NT * 8], F32))
        m1w = _es12.enter_context(nc.sbuf_tensor("m1w", [128, NT * 8], F32))
        m2w = _es12.enter_context(nc.sbuf_tensor("m2w", [128, NT * 8], F32))
        w12 = _es12.enter_context(nc.sbuf_tensor("w12", [128, 4 * NT], F32))
        ps_sco = _es12.enter_context(nc.psum_tensor("ps_sco", [8, NW * 512], F32))
        ps_str = _es12.enter_context(nc.psum_tensor("ps_str", [128, 8], F32))
        gwv = gwt[:].rearrange("p (k c) -> p k c", k=KD)

        @block.gpsimd
        def _(g):
            I(g.dma_start(gwv[:], w_gw.rearrange("(k p) c -> p k c", p=128)), "gwl", 16)

        @block.tensor
        def _(t):
            W(t, "gwl", 16)
            W(t, "x4l", 16 * KD)
            for k in range(KD):
                for w in range(NW):
                    i = t.matmul(ps_sco[:, w * 512:(w + 1) * 512], gwv[:, k],
                                 xn4v[:, k, w * 512:(w + 1) * 512],
                                 start=(k == 0), stop=(k == KD - 1))
            I(i, "te")
        te_sco = N["te"]

        @block.scalar
        def _(s):
            W(s, "te", te_sco)
            I(s.activation(sc8[:], ps_sco[0:8, 0:T], AF.Sigmoid), "ac")
        ac_sco = N["ac"]
        dv_str0 = N["dv"]

        @block.tensor
        def _(t):
            W(t, "ac", ac_sco)
            for tt in range(NT):
                if tt >= 1:
                    W(t, "dv", dv_str0 + tt)
                I(t.transpose(ps_str[:], sc8[:, tt * 128:(tt + 1) * 128], id_f32[0:8, 0:8]), "te")
        te_str = N["te"] - NT

        @block.vector
        def _(v):
            def step(i):
                I(i, "dv")
                W(v, "dv", N["dv"])
            for tt in range(NT):
                W(v, "te", te_str + tt + 1)
                I(v.tensor_copy(sctm[:, tt * 8:(tt + 1) * 8], ps_str[0:128, 0:8]), "dv")
            W(v, "dv", N["dv"])
            v3 = lambda t_: t_[:].rearrange("p (t e) -> p t e", t=NT)
            bb, _ = bass.broadcast_tensor_aps(
                bias_bc[:].rearrange("p (o e) -> p o e", o=1), v3(sb8))
            step(v.tensor_tensor(v3(sb8), v3(sctm), bb, ALU.add))
            for tt in range(NT):
                i = v.max(out=srt_w[:, tt * 8:(tt + 1) * 8], in_=sb8[:, tt * 8:(tt + 1) * 8])
            step(i)
            top1 = srt_w[:].rearrange("p (t e) -> p t e", t=NT)[:, :, 0:1]
            top2 = srt_w[:].rearrange("p (t e) -> p t e", t=NT)[:, :, 1:2]
            b1, _ = bass.broadcast_tensor_aps(top1, v3(sb8))
            b2, _ = bass.broadcast_tensor_aps(top2, v3(sb8))
            step(v.tensor_tensor(v3(m1w), v3(sb8), b1, ALU.is_equal))
            step(v.tensor_tensor(v3(m2w), v3(sb8), b2, ALU.is_equal))
            step(v.tensor_tensor(v3(srt_w), v3(sctm), v3(m1w), ALU.mult))
            step(v.tensor_reduce(w12[:, 0:NT], v3(srt_w), AX.X, ALU.add))
            step(v.tensor_tensor(v3(srt_w), v3(sctm), v3(m2w), ALU.mult))
            step(v.tensor_reduce(w12[:, NT:2 * NT], v3(srt_w), AX.X, ALU.add))
            step(v.tensor_tensor(w12[:, 2 * NT:3 * NT], w12[:, 0:NT], w12[:, NT:2 * NT],
                                 ALU.add))
            step(v.reciprocal(w12[:, 3 * NT:4 * NT], w12[:, 2 * NT:3 * NT]))
            wv1 = w12[:, 0:NT].rearrange("p (t o) -> p t o", o=1)
            wv2 = w12[:, NT:2 * NT].rearrange("p (t o) -> p t o", o=1)
            wvr = w12[:, 3 * NT:4 * NT].rearrange("p (t o) -> p t o", o=1)
            c1, _ = bass.broadcast_tensor_aps(wv1, v3(m1w))
            c2, _ = bass.broadcast_tensor_aps(wv2, v3(m2w))
            cr, _ = bass.broadcast_tensor_aps(wvr, v3(m1w))
            step(v.tensor_tensor(v3(m1w), v3(m1w), c1, ALU.mult))
            step(v.tensor_tensor(v3(m2w), v3(m2w), c2, ALU.mult))
            step(v.tensor_tensor(v3(m1w), v3(m1w), v3(m2w), ALU.add))
            step(v.tensor_tensor(v3(m1w), v3(m1w), cr, ALU.mult))
            wev = we_sb[:].rearrange("p (t e) -> p t e", t=16)
            step(v.tensor_copy(wev[:, 0:NT, 0:8], v3(m1w)))
            step(v.memset(wev[:, 0:NT, 8:NEX], 1.0))
        dv_we = N["dv"]

        if taps:
            @block.sync
            def _(sy):
                W(sy, "dv", dv_we)
                I(sy.dma_start(tp["t_we"], we_sb[:, 0:NT * NEX]), "tap", 16)

        _es12.close()
        with nc.psum_tensor("ps_up", [128, 8 * 512], F32) as ps_up:
            te_e0 = N["te"]
            dv_e0 = N["dv"]
            ac_e0 = N["ac"]
            TEE = 2 * NJ + NT
            DVE_ = NJ
            ACE = NJ + NT
            te_up_g = lambda e, j: te_e0 + e * TEE + 2 * j + 1
            te_up_u = lambda e, j: te_e0 + e * TEE + 2 * j + 2
            te_up = te_up_u
            te_dn = lambda e, tt: te_e0 + e * TEE + 2 * NJ + tt + 1
            dv_hm = lambda e, j: dv_e0 + e * DVE_ + j + 1
            ac_sil = lambda e, j: ac_e0 + e * ACE + j + 1
            ac_oev = lambda e, tt: ac_e0 + e * ACE + NJ + tt + 1
            up_st = lambda j: "upa" if j % 2 == 0 else "upb"
            ae_st = lambda e, tt: "aea" if (e * NT + tt) % 2 == 0 else "aeb"
            up_cnt, ae_cnt = {}, {}

            @block.gpsimd
            def _(g):
                for e in range(NEX):
                    if e >= 1:
                        W(g, "te", te_dn(e - 1, NT - 1))
                    for j in range(NJ):
                        I(g.dma_start(wdnv[:, j], w_dn10[e, j * 128:(j + 1) * 128, :]),
                          "dnl", 16)
                    for j in range(NJ):
                        if j >= 2:
                            W(g, "te", te_up(e, j - 2))
                        up_cnt[(e, j)] = I(g.dma_start(
                            wupv[j % 2][:],
                            w_up10[e].rearrange("(k p) c -> p k c", p=128)
                            [:, :, j * 256:(j + 1) * 256]), up_st(j), 16)
                    for tt in range(NT):
                        W(g, "ac", ac_oev(e, tt))
                        ae_cnt[(e, tt)] = I(g.dma_start(
                            acc_scr[tt * 128:(tt + 1) * 128, :], oeb[tt % 2][:],
                            accum_op=ALU.add), ae_st(e, tt), 16)

            @block.tensor
            def _(t):
                for e in range(NEX):
                    if e == 0:
                        W(t, "dv", dv_we)
                    for j in range(NJ):
                        W(t, up_st(j), up_cnt[(e, j)])
                        if j == 0 and e >= 1:
                            W(t, "ac", ac_oev(e - 1, NT - 1))
                        for half in range(2):
                            if j >= 1:
                                if half == 0:
                                    W(t, "ac", ac_sil(e, j - 1))
                                else:
                                    W(t, "dv", dv_hm(e, j - 1))
                            for k in range(KD):
                                for w in range(NW):
                                    i = t.matmul(
                                        ps_up[:, (half * NW + w) * 512:(half * NW + w + 1) * 512],
                                        wupv[j % 2][:, k, half * 128:half * 128 + 128],
                                        xn4v[:, k, w * 512:(w + 1) * 512],
                                        start=(k == 0), stop=(k == KD - 1))
                            I(i, "te")
                        assert N["te"] == te_up_u(e, j)
                    for tt in range(NT):
                        if tt == 0:
                            W(t, "dv", dv_hm(e, NJ - 1))
                            W(t, "dnl", 16 * NJ * (e + 1))
                        if tt >= 2:
                            W(t, "ac", ac_oev(e, tt - 2))
                        ps = ps_up[:, (tt % 2) * 2048:(tt % 2) * 2048 + 2048]
                        for j in range(NJ):
                            for w in range(4):
                                i = t.matmul(ps[:, w * 512:(w + 1) * 512],
                                             he[:, j * T + tt * 128:j * T + (tt + 1) * 128],
                                             wdnv[:, j, w * 512:(w + 1) * 512],
                                             start=(j == 0), stop=(j == NJ - 1))
                        I(i, "te")
                        assert N["te"] == te_dn(e, tt)

            @block.vector
            def _(v):
                for e in range(NEX):
                    for j in range(NJ):
                        W(v, "te", te_up_u(e, j))
                        W(v, "ac", ac_sil(e, j))
                        I(v.tensor_tensor(he[:, j * T:(j + 1) * T], sil[:],
                                          ps_up[:, NW * 512:2 * NW * 512], ALU.mult), "dv")

            @block.scalar
            def _(s):
                for e in range(NEX):
                    for j in range(NJ):
                        W(s, "te", te_up_g(e, j))
                        if j >= 1:
                            W(s, "dv", dv_hm(e, j - 1))
                        I(s.activation(sil[:], ps_up[:, 0:NW * 512], AF.Silu), "ac")
                    for tt in range(NT):
                        W(s, "te", te_dn(e, tt))
                        if e * NT + tt >= 2:
                            pe, pt = divmod(e * NT + tt - 2, NT)
                            W(s, ae_st(pe, pt), ae_cnt[(pe, pt)])
                        I(s.activation(oeb[tt % 2][:],
                                       ps_up[:, (tt % 2) * 2048:(tt % 2) * 2048 + 2048],
                                       AF.Identity,
                                       scale=we_sb[:, tt * NEX + e:tt * NEX + e + 1]), "ac")

        if taps:
            @block.sync
            def _(sy):
                WALL(sy, "aea", "aeb")
                I(sy.dma_start(tp["t_acc4"], acc_scr), "tap", 16)

        _es4.close()
    if phases >= 9:
        phase_barrier()
        _es5 = ExitStack()
        acb0 = _es5.enter_context(nc.sbuf_tensor("acb0", [128, D], F32))
        acb1 = _es5.enter_context(nc.sbuf_tensor("acb1", [128, D], F32))
        xr0 = _es5.enter_context(nc.sbuf_tensor("xr0", [128, D], F32))
        xr1 = _es5.enter_context(nc.sbuf_tensor("xr1", [128, D], F32))
        ou0 = _es5.enter_context(nc.sbuf_tensor("ou0", [128, D], F32))
        ou1 = _es5.enter_context(nc.sbuf_tensor("ou1", [128, D], F32))
        acb, xrb, oub = [acb0, acb1], [xr0, xr1], [ou0, ou1]
        dv0 = N["dv"]
        pl_st = lambda tt: "pla" if tt % 2 == 0 else "plb"
        st_st = lambda tt: "sta" if tt % 2 == 0 else "stb"

        @block.sync
        def _(sy):
            for tt in range(NT):
                if tt >= 2:
                    W(sy, "dv", dv0 + 2 * (tt - 1))
                I(sy.dma_start(acb[tt % 2][:], acc_scr[tt * 128:(tt + 1) * 128, :]),
                  pl_st(tt), 16)
                I(sy.dma_start(xrb[tt % 2][:], xpad[128 + tt * 128:128 + (tt + 1) * 128, :]),
                  pl_st(tt), 16)
                if tt >= 1:
                    W(sy, "dv", dv0 + 2 * tt)
                    I(sy.dma_start(out[(tt - 1) * 128:tt * 128, :], oub[(tt - 1) % 2][:]),
                      st_st(tt - 1), 16)
            W(sy, "dv", dv0 + 2 * NT)
            I(sy.dma_start(out[(NT - 1) * 128:NT * 128, :], oub[(NT - 1) % 2][:]),
              st_st(NT - 1), 16)
            WALL(sy, "sta", "stb")

        @block.vector
        def _(v):
            for tt in range(NT):
                W(v, pl_st(tt), 32 * (tt // 2 + 1))
                if tt >= 2:
                    W(v, st_st(tt - 2), 16 * ((tt - 2) // 2 + 1))
                I(v.tensor_tensor(oub[tt % 2][:], acb[tt % 2][:], xrb[tt % 2][:], ALU.add), "dv")
                W(v, "dv", N["dv"])
                I(v.engine_nop(), "dv")
        _es5.close()
    else:
        with nc.sbuf_tensor("zzz", [128, D], F32) as zzz:
            @block.sync
            def _(sy):
                I(sy.dma_start(zzz[:], xpad[0:128, :]), "pla", 16)
                W(sy, "pla", N["pla"])
                for tt in range(NT):
                    I(sy.dma_start(out[tt * 128:(tt + 1) * 128, :], zzz[:]), "sta", 16)
                W(sy, "sta", N["sta"])

    _es1.close()
    for k in N:
        assert max_waited[k] <= N[k], f"sem {k}: waited {max_waited[k]} > produced {N[k]}"
    return nc, tp


_CACHE = {}


def _prep_weights(i):
    NJl = HEF // 128
    up = i["L0_ffn_up_w"]
    dn = i["L0_ffn_down_w"]
    sgw = i["L0_ffn_shared_gw"]
    sdw = np.ascontiguousarray(i["L0_ffn_shared_dw"].T)
    up_t = np.transpose(up, (0, 2, 1))
    up_pairs = np.empty((NEX, D, 2 * HEF), np.float32)
    for e in range(8):
        for j in range(NJl):
            up_pairs[e, :, j * 256:j * 256 + 128] = up_t[e][:, j * 128:(j + 1) * 128]
            up_pairs[e, :, j * 256 + 128:(j + 1) * 256] = up_t[e][:, HEF + j * 128:HEF + (j + 1) * 128]
    sgw_t = np.ascontiguousarray(sgw.T)
    for s in range(2):
        for j in range(NJl):
            jj = s * NJl + j
            up_pairs[8 + s, :, j * 256:j * 256 + 128] = sgw_t[:, HSF + jj * 128:HSF + (jj + 1) * 128]
            up_pairs[8 + s, :, j * 256 + 128:(j + 1) * 256] = sgw_t[:, jj * 128:(jj + 1) * 128]
    dn10 = np.empty((NEX, HEF, D), np.float32)
    dn10[0:8] = np.transpose(dn, (0, 2, 1))
    dn10[8] = sdw[0:HEF]
    dn10[9] = sdw[HEF:2 * HEF]
    ipw = i["L0_attn_in_proj_w"]
    return {
        "w_gate": np.ascontiguousarray(ipw[:, 0:INTER]),
        "w_hbc": np.ascontiguousarray(ipw[:, INTER:INTER + CONV_DIM]),
        "w_dt": np.ascontiguousarray(ipw[:, INTER + CONV_DIM:]),
        "w_out": np.ascontiguousarray(i["L0_attn_out_proj_w"]),
        "w_up10": up_pairs,
        "w_dn10": dn10,
        "w_gw": np.ascontiguousarray(i["L0_ffn_gate_w"].T),
        "conv_wt": np.ascontiguousarray(i["L0_attn_conv_w"].reshape(4, CONV_DIM).T),
        "conv_bc": np.ascontiguousarray(i["L0_attn_conv_b"].reshape(CONV_DIM, 1)),
        "dt_bias": np.ascontiguousarray(i["L0_attn_dt_bias"].reshape(NH, 1)),
        "a_log": np.ascontiguousarray(i["L0_attn_A_log"].reshape(NH, 1)),
        "d_row": np.ascontiguousarray(np.repeat(np.asarray(i["L0_attn_D"]), HD).reshape(1, INTER)),
        "nw_cols": np.ascontiguousarray(np.asarray(i["L0_attn_norm_w"]).reshape(INTER // 128, 128).T),
        "gb_row": np.ascontiguousarray(np.asarray(i["L0_ffn_gate_bias"]).reshape(1, 8)),
        "ones_row": np.ones((1, 128), np.float32),
    }


def make_in_maps(inputs, T=2048):
    x = inputs["L0_attn_input_states"]
    B, Sq, _ = x.shape
    shared = _prep_weights(inputs)
    in_maps = []
    nquart = Sq // T
    for cid in range(8):
        b, q = divmod(cid, nquart)
        lo, hi = q * T - 128, (q + 1) * T + 128
        xp = np.zeros((T + 256, D), np.float32)
        s0, s1 = max(lo, 0), min(hi, Sq)
        xp[s0 - lo:s1 - lo] = x[b, s0:s1]
        incl = np.zeros((8, 1), np.float32)
        tm = np.zeros((8, 8), np.float32)
        for r in range(8):
            br, qr = divmod(r, nquart)
            if br == b and qr < q:
                incl[r, 0] = 1.0
                for mm_ in range(8):
                    bm, qm = divmod(mm_, nquart)
                    if bm == b and qr < qm < q:
                        tm[r, mm_] = 1.0
        m = dict(shared)
        m["xpad"] = xp
        m["incl_col"] = incl
        m["tm_T"] = np.ascontiguousarray(tm.T)
        in_maps.append(m)
    return in_maps


def kernel(**inputs):
    inputs = {k: np.asarray(v, dtype=np.float32) for k, v in inputs.items()}
    x = inputs["L0_attn_input_states"]
    B, Sq, _ = x.shape
    T = 2048
    in_maps = make_in_maps(inputs, T)
    if "nc" not in _CACHE:
        _CACHE["nc"] = build_program(T=T)[0]
    res = run_bass_kernel_spmd(_CACHE["nc"], in_maps, core_ids=list(range(8)))
    out = np.empty((B, Sq, D), np.float32)
    for cid in range(8):
        b, q = divmod(cid, Sq // T)
        out[b, q * T:(q + 1) * T] = np.asarray(res.results[cid]["out"], np.float32)
    return out
